# version 02
# baseline (speedup 1.0000x reference)
import os
import numpy as np
import contextlib
from collections import defaultdict

PARANOID = bool(int(os.environ.get("MK_PARANOID", "0")))
SELFSYNC = bool(int(os.environ.get("MK_SELFSYNC", "0")))
PARASEM = tuple(s for s in os.environ.get("MK_PARASEM", "").split(",") if s)

import concourse.bass as bass
import concourse.mybir as mybir
from concourse.bass_utils import run_bass_kernel_spmd

F32 = mybir.dt.float32
BF16 = mybir.dt.bfloat16
I32 = mybir.dt.int32
AF = mybir.ActivationFunctionType
ALU = mybir.AluOpType

D = 4096
KD = 32
HID = 11008
KH = 86
T = 512
NPASS = 4
TOK = T * NPASS
NCORES = 8
NH = 8
LN_EPS = 1e-5
ATT_EPS = 1e-6
QUARTERS = [(0, 22), (22, 44), (44, 65), (65, 86)]

COMPUTE_ENGINES = ("pe", "dve", "act", "pool")
DMA_QUEUES = ("spq", "gpq")
ALL_SEMS = COMPUTE_ENGINES + DMA_QUEUES
SECTION = {"pe": "tensor", "dve": "vector", "act": "scalar", "pool": "gpsimd",
           "spq": "sync", "gpq": "gpsimd"}
SM_SEMS = tuple(f"sm{i}" for i in range(8))
SEM_SECTION = {"pe": "tensor", "dve": "vector", "act": "scalar", "pool": "gpsimd",
               "spq": "sync", "gpq": "gpsimd",
               **{s: "sync" for s in SM_SEMS}}


class Res:
    __slots__ = ("w", "readers")

    def __init__(self):
        self.w = None
        self.readers = []


class Op:
    __slots__ = ("emit", "waits", "inc")

    def __init__(self, emit, waits, inc):
        self.emit = emit
        self.waits = waits
        self.inc = inc


class Sched:
    def __init__(self, nc, cnt=None):
        self.nc = nc
        self.cnt = defaultdict(int) if cnt is None else cnt
        self.ops = {s: [] for s in set(SECTION.values())}
        self.n_ops = 0
        self.phase = ""

    def res(self):
        return Res()

    def add(self, engine, emit, reads=(), writes=(), inc_amt=None, sem=None):
        if sem is None:
            sem = engine
        if inc_amt is None:
            inc_amt = 16 if engine in DMA_QUEUES else 1
        needed = {}

        def need(ev):
            if ev is None:
                return
            s, v = ev
            if s == sem and engine in COMPUTE_ENGINES:
                return
            if s in DMA_QUEUES:
                v = max(v, self.cnt[s])
            if needed.get(s, -1) < v:
                needed[s] = v

        for r in reads:
            need(r.w)
        for r in writes:
            need(r.w)
            for ev in r.readers:
                need(ev)
        if PARANOID:
            for s in ALL_SEMS:
                if self.cnt[s]:
                    need((s, self.cnt[s]))
        _pe = os.environ.get("MK_PARAENG", "")
        _ph = os.environ.get("MK_PARAPHASE", "")
        if ((not _pe) or engine in _pe.split(",")) and ((not _ph) or self.phase in _ph.split(",")):
            for s in PARASEM:
                if self.cnt[s]:
                    need((s, self.cnt[s]))
        if SELFSYNC and engine in ("dve", "act", "pool") and self.cnt[sem]:
            needed[sem] = max(needed.get(sem, 0), self.cnt[sem])
        waits = sorted(needed.items())
        self.cnt[sem] += inc_amt
        ev = (sem, self.cnt[sem])
        for r in reads:
            r.readers.append(ev)
        for r in writes:
            r.w = ev
            r.readers = []
        self.ops[SECTION[engine]].append(Op(emit, waits, (sem, inc_amt)))
        self.n_ops += 1
        return ev

    def finalize(self, block, sems, final_waits=()):
        fw = defaultdict(list)
        for s, v in final_waits:
            fw[SEM_SECTION[s]].append((s, v))

        def mk_body(section):
            ops = self.ops[section]

            def body(eng):
                waited = {}
                for op in ops:
                    for s, v in op.waits:
                        if waited.get(s, 0) >= v:
                            continue
                        eng.wait_ge(sems[s], v)
                        waited[s] = v
                    ins = op.emit(eng)
                    s, amt = op.inc
                    if ins is None:
                        ins = eng.nop()
                    ins.then_inc(sems[s], amt)
                for s, v in fw[section]:
                    eng.wait_ge(sems[s], v)
            return body

        for section, attr in (("tensor", "tensor"), ("vector", "vector"),
                              ("scalar", "scalar"), ("gpsimd", "gpsimd"),
                              ("sync", "sync")):
            if self.ops[section] or fw[section]:
                getattr(block, attr)(mk_body(section))


def build_nc(debug_taps=()):
    nc = bass.Bass("TRN2", target_bir_lowering=False)

    def inp(name, shape):
        return nc.dram_tensor(name, shape, F32, kind="ExternalInput").ap()

    hs = inp("L0_attn_hidden_states", [TOK, D])
    wq_d = inp("L0_attn_Wq", [D, D])
    bq_d = inp("L0_attn_bq", [D])
    wo_d = inp("L0_attn_Wo", [D, D])
    m_d = inp("L0_attn_M", [NH, 128, 128])
    z_d = inp("L0_attn_z", [NH, 128])
    w1_d = inp("L0_ffn_w1", [HID, D])
    w3_d = inp("L0_ffn_w3", [HID, D])
    w2_d = inp("L0_ffn_w2", [D, HID])
    a1_d = inp("L0_ffn_lora_A1", [16, D])
    b1_d = inp("L0_ffn_lora_B1", [HID, 16])
    a3_d = inp("L0_ffn_lora_A3", [16, D])
    b3_d = inp("L0_ffn_lora_B3", [HID, 16])
    a2_d = inp("L0_ffn_lora_A2", [16, HID])
    b2_d = inp("L0_ffn_lora_B2", [D, 16])
    out_d = nc.dram_tensor("out", [TOK, D], F32, kind="ExternalOutput").ap()

    w13t = nc.dram_tensor("w13t", [KH, 128, 8192], BF16).ap()
    w2t = nc.dram_tensor("w2t", [KD, 128, HID], BF16).ap()
    wqt = nc.dram_tensor("wqt", [NH, 128, D], BF16).ap()
    wot = nc.dram_tensor("wot", [KD, 128, 1024], BF16).ap()


    _TAPS = {
        "w13t0": [128, 8192], "w2t0": [128, HID],
        "wqt0": [128, 4096], "wot0": [128, 1024],
        "y1": [128, KD * T], "xf1": [128, KD * T],
        "y2": [128, KD * T], "sqv": [128, NH * T],
        "retv": [128, NH * T], "xat": [128, KD * T],
        "xf2": [128, KD * T],
    }
    dbg = {}
    for name in debug_taps:
        dbg[name] = nc.dram_tensor(name, _TAPS[name], F32, kind="ExternalOutput").ap()

    with contextlib.ExitStack() as ctx:
        E = ctx.enter_context
        x_sb = E(nc.sbuf_tensor("x_sb", [128, KD * T], F32))
        ybuf = E(nc.sbuf_tensor("ybuf", [128, KD * T], BF16))
        gbuf = E(nc.sbuf_tensor("gbuf", [128, 22 * T], BF16))
        sq = E(nc.sbuf_tensor("sq", [128, NH * T], BF16))
        arA = E(nc.sbuf_tensor("arA", [128, 12288], BF16))
        arB = E(nc.sbuf_tensor("arB", [128, 5632], BF16))
        stag = E(nc.sbuf_tensor("stag", [128, 4096], F32))
        sqtmp = E(nc.sbuf_tensor("sqtmp", [128, 2 * T], BF16))
        m_bf = E(nc.sbuf_tensor("m_bf", [128, NH * 128], BF16))
        idb = E(nc.sbuf_tensor("idb", [128, 128], BF16))
        idf = E(nc.sbuf_tensor("idf", [128, 128], F32))
        iot = E(nc.sbuf_tensor("iot", [128, 128], I32))
        zt = E(nc.sbuf_tensor("zt", [128, NH], BF16))
        bqs = E(nc.sbuf_tensor("bqs", [128, NH], F32))
        onesc = E(nc.sbuf_tensor("onesc", [128, 1], BF16))
        onesr = E(nc.sbuf_tensor("onesr", [1, 128], F32))
        eps_t = E(nc.sbuf_tensor("eps_t", [128, 1], F32))
        r_a = E(nc.sbuf_tensor("r_a", [1, T], F32))
        r_b = E(nc.sbuf_tensor("r_b", [1, T], F32))
        r_c = E(nc.sbuf_tensor("r_c", [1, T], F32))
        smt = E(nc.sbuf_tensor("smt", [16, 512], BF16))
        btmp = E(nc.sbuf_tensor("btmp", [128, 64], F32))
        btb = E(nc.sbuf_tensor("btb", [128, 64], BF16))
        psum = [E(nc.psum_tensor(f"ps{i}", [128, T], F32)) for i in range(8)]
        sems = {s: E(nc.semaphore(s)) for s in ALL_SEMS + SM_SEMS}

        cnt = defaultdict(int)

        with nc.Block("prep") as block:
            S = Sched(nc, cnt)
            r_bank = [S.res() for _ in range(8)]
            r_xsb = [S.res() for _ in range(4)]
            r_ybf = [S.res() for _ in range(4)]
            r_gb = S.res()
            r_arA = S.res()
            r_stag = S.res()
            r_smt = S.res()
            r_btmp = S.res()
            r_btb = S.res()
            r_sq = S.res()
            r_id = S.res()
            r_iot = S.res()
            r_mb = S.res()
            r_zt = S.res()
            r_bqs = S.res()
            r_ones = S.res()
            r_w13 = [S.res() for _ in range(KH)]
            r_w2 = [S.res() for _ in range(KD)]
            r_wq = [S.res() for _ in range(NH)]
            r_wo = [S.res() for _ in range(KD)]
            r_rows = S.res()

            S.add("pool", lambda e: e.iota(iot[:], [[1, 128]], base=0, channel_multiplier=-1),
                  writes=[r_iot])
            S.add("dve", lambda e: e.tensor_scalar(idb[:], iot[:], 0, None, op0=ALU.is_equal),
                  reads=[r_iot], writes=[r_id])
            S.add("dve", lambda e: e.tensor_scalar(idf[:], iot[:], 0, None, op0=ALU.is_equal),
                  reads=[r_iot], writes=[r_id])
            S.add("pool", lambda e: e.memset(onesc[:], 1.0), writes=[r_ones])
            S.add("pool", lambda e: e.memset(onesr[:], 1.0), writes=[r_ones])
            S.add("pool", lambda e: e.memset(eps_t[:], LN_EPS), writes=[r_ones])

            S.add("spq", lambda e: e.dma_start(
                stag[:, 0:1024].rearrange("p (h e) -> p h e", h=NH),
                m_d.rearrange("h d e -> d h e")), writes=[r_stag])
            S.add("dve", lambda e: e.tensor_copy(m_bf[:], stag[:, 0:1024]),
                  reads=[r_stag], writes=[r_mb])
            S.add("spq", lambda e: e.dma_start(stag[0:8, 1024:1152], z_d),
                  reads=[r_stag], writes=[r_stag], sem="sm0")
            S.add("dve", lambda e: e.tensor_copy(smt[0:8, 0:128], stag[0:8, 1024:1152]),
                  reads=[r_stag], writes=[r_smt])
            S.add("pe", lambda e: e.matmul(psum[6][:, 0:8], smt[0:8, 0:128], idb[0:8, 0:8],
                                           start=True, stop=True),
                  reads=[r_smt, r_id], writes=[r_bank[6]])
            S.add("dve", lambda e: e.tensor_copy(zt[:], psum[6][:, 0:8]),
                  reads=[r_bank[6]], writes=[r_zt])
            S.add("spq", lambda e: e.dma_start(
                stag[0:1, 1024:2048].rearrange("p (h x) -> p h x", h=NH),
                bq_d.rearrange("(a h g x) -> a h g x", a=1, h=NH, g=4)[:, :, 0, :]),
                reads=[r_stag], writes=[r_stag], sem="sm1")
            def bqtr(e):
                ins = None
                for h in range(NH):
                    ins = e.matmul(psum[7][:, h:h + 1],
                                   stag[0:1, 1024 + 128 * h:1024 + 128 * (h + 1)],
                                   idf[0:1, 0:1], start=(h == 0), stop=(h == NH - 1))
                return ins
            S.add("pe", bqtr, reads=[r_stag, r_id], writes=[r_bank[7]])
            S.add("dve", lambda e: e.tensor_copy(bqs[:], psum[7][:, 0:NH]),
                  reads=[r_bank[7]], writes=[r_bqs])

            S.add("spq", lambda e: e.dma_start(stag[0:16, :], a1_d),
                  reads=[r_stag], writes=[r_stag], sem="sm2")
            S.add("dve", lambda e: e.tensor_copy(sq[0:16, :], stag[0:16, :]),
                  reads=[r_stag], writes=[r_sq])
            S.add("spq", lambda e: e.dma_start(stag[0:16, :], a3_d),
                  reads=[r_stag], writes=[r_stag], sem="sm3")
            S.add("dve", lambda e: e.tensor_copy(arA[0:16, 8192:12288], stag[0:16, :]),
                  reads=[r_stag], writes=[r_arA])

            def prep_w13():
                for c in range(KH):
                    par = c % 2
                    xs0, xs1 = 2 * par, 2 * par + 1
                    q1, q2 = ("spq", "gpq") if par == 0 else ("gpq", "spq")
                    S.add(q1, lambda e, c=c, xs0=xs0: e.dma_start(
                        x_sb[:, xs0 * 4096:(xs0 + 1) * 4096], w1_d[c * 128:(c + 1) * 128, :]),
                        writes=[r_xsb[xs0]])
                    S.add(q2, lambda e, c=c, xs1=xs1: e.dma_start(
                        x_sb[:, xs1 * 4096:(xs1 + 1) * 4096], w3_d[c * 128:(c + 1) * 128, :]),
                        writes=[r_xsb[xs1]])
                    bt0 = 32 * par
                    S.add(q1, lambda e, c=c, bt0=bt0: e.dma_start(
                        btmp[:, bt0:bt0 + 16], b1_d[c * 128:(c + 1) * 128, :]),
                        writes=[r_btmp])
                    S.add(q2, lambda e, c=c, bt0=bt0: e.dma_start(
                        btmp[:, bt0 + 16:bt0 + 32], b3_d[c * 128:(c + 1) * 128, :]),
                        writes=[r_btmp])
                    S.add("dve", lambda e, bt0=bt0: e.tensor_scalar(
                        btb[:, bt0:bt0 + 32], btmp[:, bt0:bt0 + 32], 2.0, None, op0=ALU.mult),
                        reads=[r_btmp], writes=[r_btb])
                    bk = 6 + par
                    def btr(e, bt0=bt0, bk=bk):
                        e.matmul(psum[bk][0:16, 0:128], btb[:, bt0:bt0 + 16], idb[:],
                                 start=True, stop=False)
                        return e.matmul(psum[bk][0:16, 128:256], btb[:, bt0 + 16:bt0 + 32], idb[:],
                                        start=False, stop=True)
                    S.add("pe", btr, reads=[r_btb, r_id], writes=[r_bank[bk]])
                    sm0 = 256 * par
                    S.add("dve", lambda e, sm0=sm0, bk=bk: e.tensor_copy(
                        smt[:, sm0:sm0 + 256], psum[bk][0:16, 0:256]),
                        reads=[r_bank[bk]], writes=[r_smt])
                    S.add("act", lambda e, xs0=xs0: e.activation(
                        ybuf[:, xs0 * 4096:(xs0 + 1) * 4096], x_sb[:, xs0 * 4096:(xs0 + 1) * 4096],
                        AF.Copy), reads=[r_xsb[xs0]], writes=[r_ybf[xs0]])
                    S.add("act", lambda e, xs1=xs1: e.activation(
                        ybuf[:, xs1 * 4096:(xs1 + 1) * 4096], x_sb[:, xs1 * 4096:(xs1 + 1) * 4096],
                        AF.Copy), reads=[r_xsb[xs1]], writes=[r_ybf[xs1]])
                    ostrip = gbuf if par == 0 else arA
                    r_ostrip = r_gb if par == 0 else r_arA
                    for half in range(2):
                        ybase = (xs0 if half == 0 else xs1) * 4096
                        r_src = r_ybf[xs0 if half == 0 else xs1]
                        smbase = sm0 + 128 * half
                        if half == 0:
                            a_ap, r_aap = sq, r_sq
                            abase = 0
                        else:
                            a_ap, r_aap = arA, r_arA
                            abase = 8192
                        for kg in range(8):
                            bk2 = (kg % 4)
                            def tiles(e, ybase=ybase, kg=kg, bk2=bk2, smbase=smbase,
                                      a_ap=a_ap, abase=abase):
                                ins = None
                                for i in range(4):
                                    k = kg * 4 + i
                                    e.matmul(psum[bk2][:, i * 128:(i + 1) * 128],
                                             ybuf[:, ybase + k * 128:ybase + (k + 1) * 128],
                                             idb[:], start=(i == 0), stop=False)
                                    ins = e.matmul(psum[bk2][:, i * 128:(i + 1) * 128],
                                                   a_ap[0:16, abase + k * 128:abase + (k + 1) * 128],
                                                   smt[0:16, smbase:smbase + 128],
                                                   start=False, stop=(i == 3))
                                return ins
                            S.add("pe", tiles, reads=[r_src, r_id, r_smt, r_aap],
                                  writes=[r_bank[bk2]])
                            S.add("dve", lambda e, kg=kg, bk2=bk2, half=half,
                                  ostrip=ostrip: e.tensor_copy(
                                      ostrip[:, half * 4096 + kg * 512: half * 4096 + (kg + 1) * 512],
                                      psum[bk2][:]),
                                  reads=[r_bank[bk2]], writes=[r_ostrip])
                    S.add(q1, lambda e, c=c, ostrip=ostrip: e.dma_start(
                        w13t[c], ostrip[:, 0:8192]),
                        reads=[r_ostrip], writes=[r_w13[c]])
            prep_w13()

            for ci in range(3):
                c0, c1 = ci * 4096, min((ci + 1) * 4096, HID)
                S.add("spq", lambda e, c0=c0, c1=c1: e.dma_start(
                    stag[0:16, 0:c1 - c0], a2_d[:, c0:c1]),
                    reads=[r_stag], writes=[r_stag], sem=f"sm{4 + ci}")
                S.add("dve", lambda e, c0=c0, c1=c1: e.tensor_copy(
                    arA[0:16, c0:c1], stag[0:16, 0:c1 - c0]),
                    reads=[r_stag], writes=[r_arA])

            def prep_w2():
                for o in range(KD):
                    par = o % 2
                    q1 = "spq" if par == 0 else "gpq"
                    S.add(q1, lambda e, o=o: e.dma_start(
                        x_sb[:, 0:HID], w2_d[o * 128:(o + 1) * 128, :]),
                        writes=[r_xsb[0], r_xsb[1], r_xsb[2]])
                    S.add(q1, lambda e, o=o: e.dma_start(
                        btmp[:, 0:16], b2_d[o * 128:(o + 1) * 128, :]), writes=[r_btmp])
                    S.add("dve", lambda e: e.tensor_copy(btb[:, 0:16], btmp[:, 0:16]),
                          reads=[r_btmp], writes=[r_btb])
                    S.add("pe", lambda e: e.matmul(psum[6][0:16, 0:128], btb[:, 0:16], idb[:],
                                                   start=True, stop=True),
                          reads=[r_btb, r_id], writes=[r_bank[6]])
                    S.add("dve", lambda e: e.tensor_copy(smt[:, 0:128], psum[6][0:16, 0:128]),
                          reads=[r_bank[6]], writes=[r_smt])
                    S.add("act", lambda e: e.activation(ybuf[:, 0:HID], x_sb[:, 0:HID],
                                                        AF.Copy, scale=0.5),
                          reads=[r_xsb[0], r_xsb[1], r_xsb[2]], writes=[r_ybf[0]])
                    for half in range(2):
                        cc0, cc1 = (0, 43) if half == 0 else (43, 86)
                        for g0 in range(cc0, cc1, 4):
                            g1 = min(g0 + 4, cc1)
                            bk2 = ((g0 // 4) % 4)
                            def tiles(e, g0=g0, g1=g1, bk2=bk2):
                                ins = None
                                for i, cc in enumerate(range(g0, g1)):
                                    e.matmul(psum[bk2][:, i * 128:(i + 1) * 128],
                                             ybuf[:, cc * 128:(cc + 1) * 128], idb[:],
                                             start=(i == 0), stop=False)
                                    ins = e.matmul(psum[bk2][:, i * 128:(i + 1) * 128],
                                                   arA[0:16, cc * 128:(cc + 1) * 128],
                                                   smt[0:16, 0:128],
                                                   start=False, stop=(cc == g1 - 1))
                                return ins
                            S.add("pe", tiles, reads=[r_ybf[0], r_id, r_smt, r_arA],
                                  writes=[r_bank[bk2]])
                            S.add("dve", lambda e, g0=g0, g1=g1, bk2=bk2: e.tensor_copy(
                                gbuf[:, g0 * 128:g1 * 128], psum[bk2][:, 0:(g1 - g0) * 128]),
                                reads=[r_bank[bk2]], writes=[r_gb])
                        S.add(q1, lambda e, o=o, cc0=cc0, cc1=cc1: e.dma_start(
                            w2t[o][:, cc0 * 128:cc1 * 128], gbuf[:, cc0 * 128:cc1 * 128]),
                            reads=[r_gb], writes=[r_w2[o]])
            prep_w2()

            def prep_wq():
                for h in range(NH):
                    par = h % 2
                    q1 = "spq" if par == 0 else "gpq"
                    xs = 2 * par
                    S.add(q1, lambda e, h=h, xs=xs: e.dma_start(
                        x_sb[:, xs * 4096:(xs + 1) * 4096], wq_d[512 * h:512 * h + 128, :]),
                        writes=[r_xsb[xs]])
                    S.add("act", lambda e, xs=xs: e.activation(
                        ybuf[:, xs * 4096:(xs + 1) * 4096], x_sb[:, xs * 4096:(xs + 1) * 4096],
                        AF.Copy), reads=[r_xsb[xs]], writes=[r_ybf[xs]])
                    ostrip = gbuf if par == 0 else arA
                    r_ostrip = r_gb if par == 0 else r_arA
                    for kg in range(8):
                        bk2 = kg % 4
                        def tiles(e, xs=xs, kg=kg, bk2=bk2):
                            ins = None
                            for i in range(4):
                                k = kg * 4 + i
                                ins = e.matmul(psum[bk2][:, i * 128:(i + 1) * 128],
                                               ybuf[:, xs * 4096 + k * 128:xs * 4096 + (k + 1) * 128],
                                               idb[:], start=(i == 0), stop=(i == 3))
                            return ins
                        S.add("pe", tiles, reads=[r_ybf[xs], r_id], writes=[r_bank[bk2]])
                        S.add("dve", lambda e, kg=kg, bk2=bk2, ostrip=ostrip: e.tensor_copy(
                            ostrip[:, kg * 512:(kg + 1) * 512], psum[bk2][:]),
                            reads=[r_bank[bk2]], writes=[r_ostrip])
                    S.add(q1, lambda e, h=h, ostrip=ostrip: e.dma_start(
                        wqt[h], ostrip[:, 0:4096]), reads=[r_ostrip], writes=[r_wq[h]])
            prep_wq()

            def prep_wo():
                for o in range(KD):
                    par = o % 2
                    q1 = "spq" if par == 0 else "gpq"
                    xs = 2 * par
                    S.add(q1, lambda e, o=o, xs=xs: e.dma_start(
                        x_sb[:, xs * 4096:(xs + 1) * 4096], wo_d[o * 128:(o + 1) * 128, :]),
                        writes=[r_xsb[xs]])
                    sbase = par * 2048
                    for h in range(NH):
                        def gsum(e, xs=xs, h=h, sbase=sbase):
                            b = xs * 4096 + h * 512
                            d = sbase + h * 128
                            e.tensor_tensor(stag[:, d:d + 128], x_sb[:, b:b + 128],
                                            x_sb[:, b + 128:b + 256], op=ALU.add)
                            e.tensor_tensor(stag[:, d:d + 128], stag[:, d:d + 128],
                                            x_sb[:, b + 256:b + 384], op=ALU.add)
                            return e.tensor_tensor(stag[:, d:d + 128], stag[:, d:d + 128],
                                                   x_sb[:, b + 384:b + 512], op=ALU.add)
                        S.add("dve", gsum, reads=[r_xsb[xs]], writes=[r_stag])
                    S.add("act", lambda e, xs=xs, sbase=sbase: e.activation(
                        ybuf[:, xs * 4096:xs * 4096 + 1024], stag[:, sbase:sbase + 1024],
                        AF.Copy), reads=[r_stag], writes=[r_ybf[xs]])
                    ostrip = gbuf if par == 0 else arA
                    r_ostrip = r_gb if par == 0 else r_arA
                    for hg in range(2):
                        bk2 = hg % 4
                        def tiles(e, xs=xs, hg=hg, bk2=bk2):
                            ins = None
                            for i in range(4):
                                h = hg * 4 + i
                                ins = e.matmul(psum[bk2][:, i * 128:(i + 1) * 128],
                                               ybuf[:, xs * 4096 + h * 128:xs * 4096 + (h + 1) * 128],
                                               idb[:], start=(i == 0), stop=(i == 3))
                            return ins
                        S.add("pe", tiles, reads=[r_ybf[xs], r_id], writes=[r_bank[bk2]])
                        S.add("dve", lambda e, hg=hg, bk2=bk2, ostrip=ostrip: e.tensor_copy(
                            ostrip[:, hg * 512:(hg + 1) * 512], psum[bk2][:]),
                            reads=[r_bank[bk2]], writes=[r_ostrip])
                    S.add(q1, lambda e, o=o, ostrip=ostrip: e.dma_start(
                        wot[o], ostrip[:, 0:1024]), reads=[r_ostrip], writes=[r_wo[o]])
            prep_wo()

            S.finalize(block, sems,
                       final_waits=[(s, S.cnt[s]) for s in DMA_QUEUES + SM_SEMS if S.cnt[s]])

        with nc.Block("main") as block:
            S = Sched(nc, cnt)
            r_bank = [S.res() for _ in range(8)]
            r_x = [S.res() for _ in range(KD)]
            r_y = S.res()
            r_g = S.res()
            r_sq = [S.res() for _ in range(NH)]
            r_slotA = [S.res() for _ in range(3)]
            r_slotB = [S.res() for _ in range(2)]
            r_stag = [S.res() for _ in range(8)]
            r_sqt = [S.res() for _ in range(2)]
            r_rows = S.res()
            slotA_ctr = [0]

            def next_slotA():
                i = slotA_ctr[0] % 3
                slotA_ctr[0] += 1
                return arA[:, i * 4096:(i + 1) * 4096], r_slotA[i]

            def stslot(i):
                return stag[:, i * T:(i + 1) * T], r_stag[i]

            def emit_ln(src_is_x=True):
                S.phase = "ln"
                for k in range(KD):
                    par = k % 2
                    S.add("dve", lambda e, k=k: e.tensor_copy(
                        ybuf[:, k * T:(k + 1) * T], x_sb[:, k * T:(k + 1) * T]),
                        reads=[r_x[k]], writes=[r_y])
                    S.add("act", lambda e, k=k, par=par: e.activation(
                        sqtmp[:, par * T:(par + 1) * T], ybuf[:, k * T:(k + 1) * T], AF.Square),
                        reads=[r_y], writes=[r_sqt[par]])
                    S.add("pe", lambda e, k=k: e.matmul(
                        psum[6][0:1, :], onesc[:], ybuf[:, k * T:(k + 1) * T],
                        start=(k == 0), stop=(k == KD - 1)),
                        reads=[r_y], writes=[r_bank[6]] if k in (0, KD - 1) else [])
                    S.add("pe", lambda e, k=k, par=par: e.matmul(
                        psum[7][0:1, :], onesc[:], sqtmp[:, par * T:(par + 1) * T],
                        start=(k == 0), stop=(k == KD - 1)),
                        reads=[r_sqt[par]], writes=[r_bank[7]] if k in (0, KD - 1) else [])
                S.add("dve", lambda e: e.tensor_scalar(r_a[:], psum[6][0:1, :], -1.0 / D, None,
                                                       op0=ALU.mult),
                      reads=[r_bank[6]], writes=[r_rows])
                S.add("dve", lambda e: e.tensor_scalar(r_b[:], psum[7][0:1, :], 1.0 / D, None,
                                                       op0=ALU.mult),
                      reads=[r_bank[7]], writes=[r_rows])
                S.add("act", lambda e: e.activation(r_c[:], r_a[:], AF.Square),
                      reads=[r_rows], writes=[r_rows])
                S.add("dve", lambda e: e.tensor_tensor(r_b[:], r_b[:], r_c[:], op=ALU.subtract),
                      reads=[r_rows], writes=[r_rows])
                S.add("act", lambda e: e.activation(r_b[:], r_b[:], AF.Sqrt, bias=eps_t[0:1, :]),
                      reads=[r_rows], writes=[r_rows])
                S.add("dve", lambda e: e.reciprocal(r_b[:], r_b[:]),
                      reads=[r_rows], writes=[r_rows])
                S.add("dve", lambda e: e.tensor_tensor(r_a[:], r_a[:], r_b[:], op=ALU.mult),
                      reads=[r_rows], writes=[r_rows])
                S.add("pe", lambda e: e.matmul(psum[6][:], onesr[:], r_b[:], start=True, stop=True),
                      reads=[r_rows], writes=[r_bank[6]])
                S.add("pe", lambda e: e.matmul(psum[7][:], onesr[:], r_a[:], start=True, stop=True),
                      reads=[r_rows], writes=[r_bank[7]])
                for k in range(KD):
                    sl, r_sl = stslot(k % 2)
                    S.add("dve", lambda e, k=k, sl=sl: e.tensor_tensor(
                        sl, x_sb[:, k * T:(k + 1) * T], psum[6][:], op=ALU.mult),
                        reads=[r_x[k], r_bank[6]], writes=[r_sl])
                    S.add("dve", lambda e, k=k, sl=sl: e.tensor_tensor(
                        ybuf[:, k * T:(k + 1) * T], sl, psum[7][:], op=ALU.add),
                        reads=[r_sl, r_bank[7]], writes=[r_y])

            def emit_ffn():
                for (c0, c1) in QUARTERS:
                    S.phase = "ffnA"
                    nq = c1 - c0
                    for c in range(c0, c1):
                        par = c % 2
                        s1, r_s1 = next_slotA()
                        s3, r_s3 = next_slotA()
                        q1, q2 = ("spq", "gpq") if c % 2 == 0 else ("gpq", "spq")
                        S.add(q1, lambda e, c=c, s1=s1: e.dma_start(s1, w13t[c][:, 0:4096]),
                              writes=[r_s1])
                        S.add(q2, lambda e, c=c, s3=s3: e.dma_start(s3, w13t[c][:, 4096:8192]),
                              writes=[r_s3])
                        b1, b3 = 2 * par, 2 * par + 1
                        def mmh(e, s=None, bk=None):
                            ins = None
                            for k in range(KD):
                                ins = e.matmul(psum[bk][:], s[:, k * 128:(k + 1) * 128],
                                               ybuf[:, k * T:(k + 1) * T],
                                               start=(k == 0), stop=(k == KD - 1))
                            return ins
                        S.add("pe", lambda e, s1=s1, b1=b1: mmh(e, s=s1, bk=b1),
                              reads=[r_s1, r_y], writes=[r_bank[b1]])
                        S.add("pe", lambda e, s3=s3, b3=b3: mmh(e, s=s3, bk=b3),
                              reads=[r_s3, r_y], writes=[r_bank[b3]])
                        sl, r_sl = stslot(2 + par)
                        S.add("act", lambda e, b1=b1, sl=sl: e.activation(sl, psum[b1][:], AF.Silu),
                              reads=[r_bank[b1]], writes=[r_sl])
                        S.add("dve", lambda e, c=c, c0=c0, b3=b3, sl=sl: e.tensor_tensor(
                            gbuf[:, (c - c0) * T:(c - c0 + 1) * T], sl, psum[b3][:], op=ALU.mult),
                            reads=[r_sl, r_bank[b3]], writes=[r_g])
                    S.phase = "ffnB"
                    for o in range(KD):
                        par = o % 2
                        slot = arB[:, par * 2816:par * 2816 + nq * 128]
                        r_sl = r_slotB[par]
                        q1 = "spq" if par == 0 else "gpq"
                        S.add(q1, lambda e, o=o, c0=c0, c1=c1, slot=slot: e.dma_start(
                            slot, w2t[o][:, c0 * 128:c1 * 128]), writes=[r_sl])
                        bk = 4 + par
                        def mmo(e, slot=slot, bk=bk, nq=nq):
                            ins = None
                            for i in range(nq):
                                ins = e.matmul(psum[bk][:], slot[:, i * 128:(i + 1) * 128],
                                               gbuf[:, i * T:(i + 1) * T],
                                               start=(i == 0), stop=(i == nq - 1))
                            return ins
                        S.add("pe", mmo, reads=[r_sl, r_g], writes=[r_bank[bk]])
                        S.add("dve", lambda e, o=o, bk=bk: e.tensor_tensor(
                            x_sb[:, o * T:(o + 1) * T], x_sb[:, o * T:(o + 1) * T],
                            psum[bk][:], op=ALU.add),
                            reads=[r_bank[bk]], writes=[r_x[o]])

            def emit_attn(p=-1):
                S.phase = "attn"
                for h in range(NH):
                    par = h % 2
                    slot, r_sl = next_slotA()
                    q1 = "spq" if par == 0 else "gpq"
                    S.add(q1, lambda e, h=h, slot=slot: e.dma_start(slot, wqt[h]),
                          writes=[r_sl])
                    bk = 4 + par
                    def mmq(e, slot=slot, bk=bk):
                        ins = None
                        for k in range(KD):
                            ins = e.matmul(psum[bk][:], slot[:, k * 128:(k + 1) * 128],
                                           ybuf[:, k * T:(k + 1) * T],
                                           start=(k == 0), stop=(k == KD - 1))
                        return ins
                    S.add("pe", mmq, reads=[r_sl, r_y], writes=[r_bank[bk]])
                    smin, r_smin = stslot(0 + par)
                    srel, r_srel = stslot(2 + par)
                    sexp, r_sexp = stslot(4 + par)
                    S.add("dve", lambda e, h=h, bk=bk, smin=smin: e.tensor_scalar(
                        smin, psum[bk][:], bqs[:, h:h + 1], 0.0, op0=ALU.add, op1=ALU.min),
                        reads=[r_bank[bk]], writes=[r_smin])
                    S.add("dve", lambda e, h=h, bk=bk, srel=srel: e.tensor_scalar(
                        srel, psum[bk][:], bqs[:, h:h + 1], 0.0, op0=ALU.add, op1=ALU.max),
                        reads=[r_bank[bk]], writes=[r_srel])
                    S.add("act", lambda e, smin=smin, sexp=sexp: e.activation(sexp, smin, AF.Exp),
                          reads=[r_smin], writes=[r_sexp])
                    S.add("dve", lambda e, h=h, sexp=sexp, srel=srel: e.tensor_tensor(
                        sq[:, h * T:(h + 1) * T], sexp, srel, op=ALU.add),
                        reads=[r_sexp, r_srel], writes=[r_sq[h]])
                tap(p, "sqv", sq[:], [r_sq[h] for h in range(NH)])
                for h in range(NH):
                    par = h % 2
                    bkd = 4 + par
                    S.add("pe", lambda e, h=h: e.matmul(
                        psum[6][:], m_bf[:, h * 128:(h + 1) * 128], sq[:, h * T:(h + 1) * T],
                        start=True, stop=True),
                        reads=[r_sq[h]], writes=[r_bank[6]])
                    S.add("pe", lambda e, h=h, bkd=bkd: e.matmul(
                        psum[bkd][0:1, :], zt[:, h:h + 1], sq[:, h * T:(h + 1) * T],
                        start=True, stop=True),
                        reads=[r_sq[h]], writes=[r_bank[bkd]])
                    S.add("dve", lambda e, bkd=bkd: e.tensor_scalar(
                        r_c[:], psum[bkd][0:1, :], ATT_EPS, None, op0=ALU.add),
                        reads=[r_bank[bkd]], writes=[r_rows])
                    S.add("dve", lambda e: e.reciprocal(r_c[:], r_c[:]),
                          reads=[r_rows], writes=[r_rows])
                    S.add("pe", lambda e: e.matmul(psum[7][:], onesr[:], r_c[:],
                                                   start=True, stop=True),
                          reads=[r_rows], writes=[r_bank[7]])
                    sdr, r_sdr = stslot(6 + par)
                    S.add("dve", lambda e, sdr=sdr: e.tensor_copy(sdr, psum[7][:]),
                          reads=[r_bank[7]], writes=[r_sdr])
                    S.add("dve", lambda e, h=h, sdr=sdr: e.tensor_tensor(
                        sq[:, h * T:(h + 1) * T], psum[6][:], sdr, op=ALU.mult),
                        reads=[r_bank[6], r_sdr], writes=[r_sq[h]])
                tap(p, "retv", sq[:], [r_sq[h] for h in range(NH)])
                for ob in range(8):
                    slot, r_sl = next_slotA()
                    q1 = "spq" if ob % 2 == 0 else "gpq"
                    S.add(q1, lambda e, ob=ob, slot=slot: e.dma_start(
                        slot.rearrange("p (o j) -> p o j", o=4),
                        wot[ob * 4:(ob + 1) * 4].rearrange("o p j -> p o j")),
                        writes=[r_sl])
                    for i in range(4):
                        o = ob * 4 + i
                        bk = 4 + (o % 2)
                        def mmo(e, slot=slot, i=i, bk=bk):
                            ins = None
                            for h in range(NH):
                                ins = e.matmul(psum[bk][:],
                                               slot[:, i * 1024 + h * 128:i * 1024 + (h + 1) * 128],
                                               sq[:, h * T:(h + 1) * T],
                                               start=(h == 0), stop=(h == NH - 1))
                            return ins
                        S.add("pe", mmo, reads=[r_sl] + [r_sq[h] for h in range(NH)],
                              writes=[r_bank[bk]])
                        S.add("dve", lambda e, o=o, bk=bk: e.tensor_tensor(
                            x_sb[:, o * T:(o + 1) * T], x_sb[:, o * T:(o + 1) * T],
                            psum[bk][:], op=ALU.add),
                            reads=[r_bank[bk]], writes=[r_x[o]])

            def tap(p, name, src, reads):
                if p != 0 or name not in dbg:
                    return
                S.add("gpq", lambda e: e.dma_start(dbg[name], src), reads=list(reads))

            def emit_pass(p):
                S.phase = "io"
                tok0 = p * T
                if p == 0:
                    tap(p, "w13t0", w13t[0], [])
                    tap(p, "w2t0", w2t[0], [])
                    tap(p, "wqt0", wqt[0], [])
                    tap(p, "wot0", wot[0], [])
                for tt in range(4):
                    S.add("gpq", lambda e, tt=tt: e.dma_start(
                        stag[:], hs[tok0 + tt * 128: tok0 + (tt + 1) * 128, :]),
                        reads=[r_stag[i] for i in range(8)],
                        writes=[r_stag[i] for i in range(8)])
                    for kg in range(8):
                        bk = 4 + (kg % 2)
                        def trp(e, kg=kg, bk=bk):
                            ins = None
                            for i in range(4):
                                k = kg * 4 + i
                                ins = e.matmul(psum[bk][:, i * 128:(i + 1) * 128],
                                               stag[:, k * 128:(k + 1) * 128], idf[:],
                                               start=(i == 0), stop=(i == 3))
                            return ins
                        S.add("pe", trp, reads=[r_stag[i] for i in range(8)],
                              writes=[r_bank[bk]])
                        for i in range(4):
                            k = kg * 4 + i
                            S.add("dve", lambda e, tt=tt, k=k, i=i, bk=bk: e.tensor_copy(
                                x_sb[:, k * T + tt * 128:k * T + (tt + 1) * 128],
                                psum[bk][:, i * 128:(i + 1) * 128]),
                                reads=[r_bank[bk]], writes=[r_x[k]])
                emit_ln()
                tap(p, "y1", ybuf[:], [r_y])
                emit_ffn()
                tap(p, "xf1", x_sb[:], [r_x[k] for k in range(KD)])
                emit_ln()
                tap(p, "y2", ybuf[:], [r_y])
                emit_attn(p)
                tap(p, "xat", x_sb[:], [r_x[k] for k in range(KD)])
                emit_ln()
                emit_ffn()
                tap(p, "xf2", x_sb[:], [r_x[k] for k in range(KD)])
                S.phase = "io"
                for tt in range(4):
                    for kg in range(8):
                        bk = 4 + (kg % 2)
                        def trp(e, tt=tt, kg=kg, bk=bk):
                            ins = None
                            for i in range(4):
                                k = kg * 4 + i
                                ins = e.matmul(psum[bk][:, i * 128:(i + 1) * 128],
                                               x_sb[:, k * T + tt * 128:k * T + (tt + 1) * 128],
                                               idf[:], start=(i == 0), stop=(i == 3))
                            return ins
                        S.add("pe", trp, reads=[r_x[kg * 4 + i] for i in range(4)],
                              writes=[r_bank[bk]])
                        S.add("dve", lambda e, kg=kg, bk=bk: e.tensor_copy(
                            stag[:, kg * 512:(kg + 1) * 512], psum[bk][:]),
                            reads=[r_bank[bk]], writes=[r_stag[kg]])
                    S.add("gpq" if tt % 2 == 0 else "spq",
                          lambda e, tt=tt: e.dma_start(
                              out_d[tok0 + tt * 128: tok0 + (tt + 1) * 128, :], stag[:]),
                          reads=[r_stag[i] for i in range(8)])

            for p in range(NPASS):
                emit_pass(p)

            S.finalize(block, sems,
                       final_waits=[(s, S.cnt[s]) for s in DMA_QUEUES if S.cnt[s]])

    return nc


WEIGHT_NAMES = [
    "L0_attn_Wq", "L0_attn_bq", "L0_attn_Wo", "L0_attn_M", "L0_attn_z",
    "L0_ffn_w1", "L0_ffn_w3", "L0_ffn_w2",
    "L0_ffn_lora_A1", "L0_ffn_lora_B1", "L0_ffn_lora_A3", "L0_ffn_lora_B3",
    "L0_ffn_lora_A2", "L0_ffn_lora_B2",
]

_NC_CACHE = {}


def kernel(**inputs):
    hs = np.ascontiguousarray(
        np.asarray(inputs["L0_attn_hidden_states"], dtype=np.float32).reshape(NCORES * TOK, D))
    if "nc" not in _NC_CACHE:
        _NC_CACHE["nc"] = build_nc()
    nc = _NC_CACHE["nc"]
    weights = {n: np.ascontiguousarray(np.asarray(inputs[n], np.float32)) for n in WEIGHT_NAMES}
    in_maps = []
    for c in range(NCORES):
        m = dict(weights)
        m["L0_attn_hidden_states"] = np.ascontiguousarray(hs[c * TOK:(c + 1) * TOK])
        in_maps.append(m)
    res = run_bass_kernel_spmd(nc, in_maps, core_ids=list(range(NCORES)))
    out = np.concatenate([np.asarray(r["out"]) for r in res.results], axis=0)
    return out.reshape(2, NCORES * TOK // 2, D)


# version 03
# speedup vs baseline: 1.1429x; 1.1429x over previous
import os
import numpy as np
import contextlib
from collections import defaultdict

PARANOID = bool(int(os.environ.get("MK_PARANOID", "0")))
SELFSYNC = bool(int(os.environ.get("MK_SELFSYNC", "0")))
PARASEM = tuple(s for s in os.environ.get("MK_PARASEM", "").split(",") if s)

import concourse.bass as bass
import concourse.mybir as mybir
from concourse.bass_utils import run_bass_kernel_spmd

F32 = mybir.dt.float32
BF16 = mybir.dt.bfloat16
I32 = mybir.dt.int32
AF = mybir.ActivationFunctionType
ALU = mybir.AluOpType

D = 4096
KD = 32
HID = 11008
KH = 86
T = 512
NPASS = 4
TOK = T * NPASS
NCORES = 8
NH = 8
LN_EPS = 1e-5
ATT_EPS = 1e-6
QUARTERS = [(0, 22), (22, 44), (44, 65), (65, 86)]

COMPUTE_ENGINES = ("pe", "dve", "act", "pool")
DMA_QUEUES = ("spq", "gpq")
ALL_SEMS = COMPUTE_ENGINES + DMA_QUEUES
SECTION = {"pe": "tensor", "dve": "vector", "act": "scalar", "pool": "gpsimd",
           "spq": "sync", "gpq": "gpsimd"}
SM_SEMS = tuple(f"sm{i}" for i in range(8))
SEM_SECTION = {"pe": "tensor", "dve": "vector", "act": "scalar", "pool": "gpsimd",
               "spq": "sync", "gpq": "gpsimd",
               **{s: "sync" for s in SM_SEMS}}


class Res:
    __slots__ = ("w", "readers")

    def __init__(self):
        self.w = None
        self.readers = []


class Op:
    __slots__ = ("emit", "waits", "inc")

    def __init__(self, emit, waits, inc):
        self.emit = emit
        self.waits = waits
        self.inc = inc


class Sched:
    def __init__(self, nc, cnt=None):
        self.nc = nc
        self.cnt = defaultdict(int) if cnt is None else cnt
        self.ops = {s: [] for s in set(SECTION.values())}
        self.n_ops = 0
        self.phase = ""

    def res(self):
        return Res()

    def add(self, engine, emit, reads=(), writes=(), inc_amt=None, sem=None):
        if sem is None:
            sem = engine
        if inc_amt is None:
            inc_amt = 16 if engine in DMA_QUEUES else 1
        needed = {}

        def need(ev):
            if ev is None:
                return
            s, v = ev
            if s == sem and engine in COMPUTE_ENGINES:
                return
            if s in DMA_QUEUES:
                v = max(v, self.cnt[s])
            if needed.get(s, -1) < v:
                needed[s] = v

        for r in reads:
            need(r.w)
        for r in writes:
            need(r.w)
            for ev in r.readers:
                need(ev)
        if PARANOID:
            for s in ALL_SEMS:
                if self.cnt[s]:
                    need((s, self.cnt[s]))
        _pe = os.environ.get("MK_PARAENG", "")
        _ph = os.environ.get("MK_PARAPHASE", "")
        if ((not _pe) or engine in _pe.split(",")) and ((not _ph) or self.phase in _ph.split(",")):
            for s in PARASEM:
                if self.cnt[s]:
                    need((s, self.cnt[s]))
        if SELFSYNC and engine in ("dve", "act", "pool") and self.cnt[sem]:
            needed[sem] = max(needed.get(sem, 0), self.cnt[sem])
        waits = sorted(needed.items())
        self.cnt[sem] += inc_amt
        ev = (sem, self.cnt[sem])
        for r in reads:
            r.readers.append(ev)
        for r in writes:
            r.w = ev
            r.readers = []
        self.ops[SECTION[engine]].append(Op(emit, waits, (sem, inc_amt)))
        self.n_ops += 1
        return ev

    def finalize(self, block, sems, final_waits=()):
        fw = defaultdict(list)
        for s, v in final_waits:
            fw[SEM_SECTION[s]].append((s, v))

        def mk_body(section):
            ops = self.ops[section]

            def body(eng):
                waited = {}
                for op in ops:
                    for s, v in op.waits:
                        if waited.get(s, 0) >= v:
                            continue
                        eng.wait_ge(sems[s], v)
                        waited[s] = v
                    ins = op.emit(eng)
                    s, amt = op.inc
                    if ins is None:
                        ins = eng.nop()
                    ins.then_inc(sems[s], amt)
                for s, v in fw[section]:
                    eng.wait_ge(sems[s], v)
            return body

        for section, attr in (("tensor", "tensor"), ("vector", "vector"),
                              ("scalar", "scalar"), ("gpsimd", "gpsimd"),
                              ("sync", "sync")):
            if self.ops[section] or fw[section]:
                getattr(block, attr)(mk_body(section))


def build_nc(debug_taps=()):
    nc = bass.Bass("TRN2", target_bir_lowering=False)

    def inp(name, shape):
        return nc.dram_tensor(name, shape, F32, kind="ExternalInput").ap()

    hs = inp("L0_attn_hidden_states", [TOK, D])
    wq_d = inp("L0_attn_Wq", [D, D])
    bq_d = inp("L0_attn_bq", [D])
    wo_d = inp("L0_attn_Wo", [D, D])
    m_d = inp("L0_attn_M", [NH, 128, 128])
    z_d = inp("L0_attn_z", [NH, 128])
    w1_d = inp("L0_ffn_w1", [HID, D])
    w3_d = inp("L0_ffn_w3", [HID, D])
    w2_d = inp("L0_ffn_w2", [D, HID])
    a1_d = inp("L0_ffn_lora_A1", [16, D])
    b1_d = inp("L0_ffn_lora_B1", [HID, 16])
    a3_d = inp("L0_ffn_lora_A3", [16, D])
    b3_d = inp("L0_ffn_lora_B3", [HID, 16])
    a2_d = inp("L0_ffn_lora_A2", [16, HID])
    b2_d = inp("L0_ffn_lora_B2", [D, 16])
    out_d = nc.dram_tensor("out", [TOK, D], F32, kind="ExternalOutput").ap()

    w13t = nc.dram_tensor("w13t", [KH, 128, 8192], BF16).ap()
    w2t = nc.dram_tensor("w2t", [KD, 128, HID], BF16).ap()
    wqt = nc.dram_tensor("wqt", [NH, 128, D], BF16).ap()
    wot = nc.dram_tensor("wot", [KD, 128, 1024], BF16).ap()


    _TAPS = {
        "w13t0": [128, 8192], "w2t0": [128, HID],
        "wqt0": [128, 4096], "wot0": [128, 1024],
        "y1": [128, KD * T], "xf1": [128, KD * T],
        "y2": [128, KD * T], "sqv": [128, NH * T],
        "retv": [128, NH * T], "xat": [128, KD * T],
        "xf2": [128, KD * T],
    }
    dbg = {}
    for name in debug_taps:
        dbg[name] = nc.dram_tensor(name, _TAPS[name], F32, kind="ExternalOutput").ap()

    with contextlib.ExitStack() as ctx:
        E = ctx.enter_context
        x_sb = E(nc.sbuf_tensor("x_sb", [128, KD * T], F32))
        ybuf = E(nc.sbuf_tensor("ybuf", [128, KD * T], BF16))
        gbuf = E(nc.sbuf_tensor("gbuf", [128, 22 * T], BF16))
        sq = E(nc.sbuf_tensor("sq", [128, NH * T], BF16))
        arA = E(nc.sbuf_tensor("arA", [128, 12288], BF16))
        arB = E(nc.sbuf_tensor("arB", [128, 8448], BF16))
        stag = E(nc.sbuf_tensor("stag", [128, 4096], F32))
        sqtmp = E(nc.sbuf_tensor("sqtmp", [128, 2 * T], BF16))
        m_bf = E(nc.sbuf_tensor("m_bf", [128, NH * 128], BF16))
        idb = E(nc.sbuf_tensor("idb", [128, 128], BF16))
        idf = E(nc.sbuf_tensor("idf", [128, 128], F32))
        iot = E(nc.sbuf_tensor("iot", [128, 128], I32))
        zt = E(nc.sbuf_tensor("zt", [128, NH], BF16))
        bqs = E(nc.sbuf_tensor("bqs", [128, NH], F32))
        onesc = E(nc.sbuf_tensor("onesc", [128, 1], BF16))
        onesr = E(nc.sbuf_tensor("onesr", [1, 128], F32))
        eps_t = E(nc.sbuf_tensor("eps_t", [128, 1], F32))
        r_a = E(nc.sbuf_tensor("r_a", [1, T], F32))
        r_b = E(nc.sbuf_tensor("r_b", [1, T], F32))
        r_c = E(nc.sbuf_tensor("r_c", [1, T], F32))
        smt = E(nc.sbuf_tensor("smt", [16, 1024], BF16))
        btmp = E(nc.sbuf_tensor("btmp", [128, 128], F32))
        btb = E(nc.sbuf_tensor("btb", [128, 128], BF16))
        psum = [E(nc.psum_tensor(f"ps{i}", [128, T], F32)) for i in range(8)]
        sems = {s: E(nc.semaphore(s)) for s in ALL_SEMS + SM_SEMS}

        cnt = defaultdict(int)

        with nc.Block("prep") as block:
            S = Sched(nc, cnt)
            r_bank = [S.res() for _ in range(8)]
            r_xsb = [S.res() for _ in range(4)]
            r_ybf = [S.res() for _ in range(4)]
            r_gb = S.res()
            r_arA = S.res()
            r_stag = S.res()
            r_smt = [S.res() for _ in range(4)]
            r_btmp = [S.res() for _ in range(4)]
            r_btb = [S.res() for _ in range(4)]
            r_sq = S.res()
            r_id = S.res()
            r_iot = S.res()
            r_mb = S.res()
            r_zt = S.res()
            r_bqs = S.res()
            r_ones = S.res()
            r_w13 = [S.res() for _ in range(KH)]
            r_w2 = [S.res() for _ in range(KD)]
            r_wq = [S.res() for _ in range(NH)]
            r_wo = [S.res() for _ in range(KD)]
            r_rows = S.res()

            S.add("pool", lambda e: e.iota(iot[:], [[1, 128]], base=0, channel_multiplier=-1),
                  writes=[r_iot])
            S.add("dve", lambda e: e.tensor_scalar(idb[:], iot[:], 0, None, op0=ALU.is_equal),
                  reads=[r_iot], writes=[r_id])
            S.add("dve", lambda e: e.tensor_scalar(idf[:], iot[:], 0, None, op0=ALU.is_equal),
                  reads=[r_iot], writes=[r_id])
            S.add("pool", lambda e: e.memset(onesc[:], 1.0), writes=[r_ones])
            S.add("pool", lambda e: e.memset(onesr[:], 1.0), writes=[r_ones])
            S.add("pool", lambda e: e.memset(eps_t[:], LN_EPS), writes=[r_ones])

            S.add("spq", lambda e: e.dma_start(
                stag[:, 0:1024].rearrange("p (h e) -> p h e", h=NH),
                m_d.rearrange("h d e -> d h e")), writes=[r_stag])
            S.add("dve", lambda e: e.tensor_copy(m_bf[:], stag[:, 0:1024]),
                  reads=[r_stag], writes=[r_mb])
            S.add("spq", lambda e: e.dma_start(stag[0:8, 1024:1152], z_d),
                  reads=[r_stag], writes=[r_stag], sem="sm0")
            S.add("dve", lambda e: e.tensor_copy(smt[0:8, 0:128], stag[0:8, 1024:1152]),
                  reads=[r_stag], writes=[r_smt[0]])
            S.add("pe", lambda e: e.matmul(psum[6][:, 0:8], smt[0:8, 0:128], idb[0:8, 0:8],
                                           start=True, stop=True),
                  reads=[r_smt[0], r_id], writes=[r_bank[6]])
            S.add("dve", lambda e: e.tensor_copy(zt[:], psum[6][:, 0:8]),
                  reads=[r_bank[6]], writes=[r_zt])
            S.add("spq", lambda e: e.dma_start(
                stag[0:1, 1024:2048].rearrange("p (h x) -> p h x", h=NH),
                bq_d.rearrange("(a h g x) -> a h g x", a=1, h=NH, g=4)[:, :, 0, :]),
                reads=[r_stag], writes=[r_stag], sem="sm1")
            def bqtr(e):
                ins = None
                for h in range(NH):
                    ins = e.matmul(psum[7][:, h:h + 1],
                                   stag[0:1, 1024 + 128 * h:1024 + 128 * (h + 1)],
                                   idf[0:1, 0:1], start=(h == 0), stop=(h == NH - 1))
                return ins
            S.add("pe", bqtr, reads=[r_stag, r_id], writes=[r_bank[7]])
            S.add("dve", lambda e: e.tensor_copy(bqs[:], psum[7][:, 0:NH]),
                  reads=[r_bank[7]], writes=[r_bqs])

            S.add("spq", lambda e: e.dma_start(stag[0:16, :], a1_d),
                  reads=[r_stag], writes=[r_stag], sem="sm2")
            S.add("dve", lambda e: e.tensor_copy(sq[0:16, :], stag[0:16, :]),
                  reads=[r_stag], writes=[r_sq])
            S.add("spq", lambda e: e.dma_start(stag[0:16, :], a3_d),
                  reads=[r_stag], writes=[r_stag], sem="sm3")
            S.add("dve", lambda e: e.tensor_copy(arA[0:16, 8192:12288], stag[0:16, :]),
                  reads=[r_stag], writes=[r_arA])

            def prep_w13():
                for c in range(KH):
                    par = c % 2
                    par4 = c % 4
                    xs0, xs1 = 2 * par, 2 * par + 1
                    S.add("spq", lambda e, c=c, xs0=xs0: e.dma_start(
                        x_sb[:, xs0 * 4096:(xs0 + 1) * 4096], w1_d[c * 128:(c + 1) * 128, :]),
                        writes=[r_xsb[xs0]])
                    S.add("spq", lambda e, c=c, xs1=xs1: e.dma_start(
                        x_sb[:, xs1 * 4096:(xs1 + 1) * 4096], w3_d[c * 128:(c + 1) * 128, :]),
                        writes=[r_xsb[xs1]])
                    bt0 = 32 * par4
                    S.add("spq", lambda e, c=c, bt0=bt0: e.dma_start(
                        btmp[:, bt0:bt0 + 16], b1_d[c * 128:(c + 1) * 128, :]),
                        writes=[r_btmp[par4]])
                    S.add("spq", lambda e, c=c, bt0=bt0: e.dma_start(
                        btmp[:, bt0 + 16:bt0 + 32], b3_d[c * 128:(c + 1) * 128, :]),
                        writes=[r_btmp[par4]])
                    S.add("dve", lambda e, bt0=bt0: e.tensor_scalar(
                        btb[:, bt0:bt0 + 32], btmp[:, bt0:bt0 + 32], 2.0, None, op0=ALU.mult),
                        reads=[r_btmp[par4]], writes=[r_btb[par4]])
                    bk = 6 + par
                    def btr(e, bt0=bt0, bk=bk):
                        e.matmul(psum[bk][0:16, 0:128], btb[:, bt0:bt0 + 16], idb[:],
                                 start=True, stop=False)
                        return e.matmul(psum[bk][0:16, 128:256], btb[:, bt0 + 16:bt0 + 32], idb[:],
                                        start=False, stop=True)
                    S.add("pe", btr, reads=[r_btb[par4], r_id], writes=[r_bank[bk]])
                    sm0 = 256 * par4
                    S.add("dve", lambda e, sm0=sm0, bk=bk: e.tensor_copy(
                        smt[:, sm0:sm0 + 256], psum[bk][0:16, 0:256]),
                        reads=[r_bank[bk]], writes=[r_smt[par4]])
                    S.add("act", lambda e, xs0=xs0: e.activation(
                        ybuf[:, xs0 * 4096:(xs0 + 1) * 4096], x_sb[:, xs0 * 4096:(xs0 + 1) * 4096],
                        AF.Copy), reads=[r_xsb[xs0]], writes=[r_ybf[xs0]])
                    S.add("act", lambda e, xs1=xs1: e.activation(
                        ybuf[:, xs1 * 4096:(xs1 + 1) * 4096], x_sb[:, xs1 * 4096:(xs1 + 1) * 4096],
                        AF.Copy), reads=[r_xsb[xs1]], writes=[r_ybf[xs1]])
                    ostrip = gbuf if par == 0 else arA
                    r_ostrip = r_gb if par == 0 else r_arA
                    for half in range(2):
                        ybase = (xs0 if half == 0 else xs1) * 4096
                        r_src = r_ybf[xs0 if half == 0 else xs1]
                        smbase = sm0 + 128 * half
                        if half == 0:
                            a_ap, r_aap = sq, r_sq
                            abase = 0
                        else:
                            a_ap, r_aap = arA, r_arA
                            abase = 8192
                        for kg in range(8):
                            bk2 = (kg % 4)
                            def tiles(e, ybase=ybase, kg=kg, bk2=bk2, smbase=smbase,
                                      a_ap=a_ap, abase=abase):
                                ins = None
                                for i in range(4):
                                    k = kg * 4 + i
                                    e.matmul(psum[bk2][:, i * 128:(i + 1) * 128],
                                             ybuf[:, ybase + k * 128:ybase + (k + 1) * 128],
                                             idb[:], start=(i == 0), stop=False)
                                    ins = e.matmul(psum[bk2][:, i * 128:(i + 1) * 128],
                                                   a_ap[0:16, abase + k * 128:abase + (k + 1) * 128],
                                                   smt[0:16, smbase:smbase + 128],
                                                   start=False, stop=(i == 3))
                                return ins
                            S.add("pe", tiles, reads=[r_src, r_id, r_smt[par4], r_aap],
                                  writes=[r_bank[bk2]])
                            S.add("dve", lambda e, kg=kg, bk2=bk2, half=half,
                                  ostrip=ostrip: e.tensor_copy(
                                      ostrip[:, half * 4096 + kg * 512: half * 4096 + (kg + 1) * 512],
                                      psum[bk2][:]),
                                  reads=[r_bank[bk2]], writes=[r_ostrip])
                    S.add("gpq", lambda e, c=c, ostrip=ostrip: e.dma_start(
                        w13t[c], ostrip[:, 0:8192]),
                        reads=[r_ostrip], writes=[r_w13[c]])
            prep_w13()

            for ci in range(3):
                c0, c1 = ci * 4096, min((ci + 1) * 4096, HID)
                S.add("spq", lambda e, c0=c0, c1=c1: e.dma_start(
                    stag[0:16, 0:c1 - c0], a2_d[:, c0:c1]),
                    reads=[r_stag], writes=[r_stag], sem=f"sm{4 + ci}")
                S.add("dve", lambda e, c0=c0, c1=c1: e.tensor_copy(
                    arA[0:16, c0:c1], stag[0:16, 0:c1 - c0]),
                    reads=[r_stag], writes=[r_arA])

            def prep_w2():
                for o in range(KD):
                    S.add("spq", lambda e, o=o: e.dma_start(
                        x_sb[:, 0:HID], w2_d[o * 128:(o + 1) * 128, :]),
                        writes=[r_xsb[0], r_xsb[1], r_xsb[2]])
                    S.add("spq", lambda e, o=o: e.dma_start(
                        btmp[:, 0:16], b2_d[o * 128:(o + 1) * 128, :]), writes=[r_btmp[0]])
                    S.add("dve", lambda e: e.tensor_copy(btb[:, 0:16], btmp[:, 0:16]),
                          reads=[r_btmp[0]], writes=[r_btb[0]])
                    S.add("pe", lambda e: e.matmul(psum[6][0:16, 0:128], btb[:, 0:16], idb[:],
                                                   start=True, stop=True),
                          reads=[r_btb[0], r_id], writes=[r_bank[6]])
                    S.add("dve", lambda e: e.tensor_copy(smt[:, 0:128], psum[6][0:16, 0:128]),
                          reads=[r_bank[6]], writes=[r_smt[0]])
                    S.add("act", lambda e: e.activation(ybuf[:, 0:HID], x_sb[:, 0:HID],
                                                        AF.Copy, scale=0.5),
                          reads=[r_xsb[0], r_xsb[1], r_xsb[2]], writes=[r_ybf[0]])
                    for half in range(2):
                        cc0, cc1 = (0, 43) if half == 0 else (43, 86)
                        for g0 in range(cc0, cc1, 4):
                            g1 = min(g0 + 4, cc1)
                            bk2 = ((g0 // 4) % 4)
                            def tiles(e, g0=g0, g1=g1, bk2=bk2):
                                ins = None
                                for i, cc in enumerate(range(g0, g1)):
                                    e.matmul(psum[bk2][:, i * 128:(i + 1) * 128],
                                             ybuf[:, cc * 128:(cc + 1) * 128], idb[:],
                                             start=(i == 0), stop=False)
                                    ins = e.matmul(psum[bk2][:, i * 128:(i + 1) * 128],
                                                   arA[0:16, cc * 128:(cc + 1) * 128],
                                                   smt[0:16, 0:128],
                                                   start=False, stop=(cc == g1 - 1))
                                return ins
                            S.add("pe", tiles, reads=[r_ybf[0], r_id, r_smt[0], r_arA],
                                  writes=[r_bank[bk2]])
                            S.add("dve", lambda e, g0=g0, g1=g1, bk2=bk2: e.tensor_copy(
                                gbuf[:, g0 * 128:g1 * 128], psum[bk2][:, 0:(g1 - g0) * 128]),
                                reads=[r_bank[bk2]], writes=[r_gb])
                        S.add("gpq", lambda e, o=o, cc0=cc0, cc1=cc1: e.dma_start(
                            w2t[o][:, cc0 * 128:cc1 * 128], gbuf[:, cc0 * 128:cc1 * 128]),
                            reads=[r_gb], writes=[r_w2[o]])
            prep_w2()

            def prep_wq():
                for h in range(NH):
                    par = h % 2
                    xs = 2 * par
                    S.add("spq", lambda e, h=h, xs=xs: e.dma_start(
                        x_sb[:, xs * 4096:(xs + 1) * 4096], wq_d[512 * h:512 * h + 128, :]),
                        writes=[r_xsb[xs]])
                    S.add("act", lambda e, xs=xs: e.activation(
                        ybuf[:, xs * 4096:(xs + 1) * 4096], x_sb[:, xs * 4096:(xs + 1) * 4096],
                        AF.Copy), reads=[r_xsb[xs]], writes=[r_ybf[xs]])
                    ostrip = gbuf if par == 0 else arA
                    r_ostrip = r_gb if par == 0 else r_arA
                    for kg in range(8):
                        bk2 = kg % 4
                        def tiles(e, xs=xs, kg=kg, bk2=bk2):
                            ins = None
                            for i in range(4):
                                k = kg * 4 + i
                                ins = e.matmul(psum[bk2][:, i * 128:(i + 1) * 128],
                                               ybuf[:, xs * 4096 + k * 128:xs * 4096 + (k + 1) * 128],
                                               idb[:], start=(i == 0), stop=(i == 3))
                            return ins
                        S.add("pe", tiles, reads=[r_ybf[xs], r_id], writes=[r_bank[bk2]])
                        S.add("dve", lambda e, kg=kg, bk2=bk2, ostrip=ostrip: e.tensor_copy(
                            ostrip[:, kg * 512:(kg + 1) * 512], psum[bk2][:]),
                            reads=[r_bank[bk2]], writes=[r_ostrip])
                    S.add("gpq", lambda e, h=h, ostrip=ostrip: e.dma_start(
                        wqt[h], ostrip[:, 0:4096]), reads=[r_ostrip], writes=[r_wq[h]])
            prep_wq()

            def prep_wo():
                for o in range(KD):
                    par = o % 2
                    xs = 2 * par
                    S.add("spq", lambda e, o=o, xs=xs: e.dma_start(
                        x_sb[:, xs * 4096:(xs + 1) * 4096], wo_d[o * 128:(o + 1) * 128, :]),
                        writes=[r_xsb[xs]])
                    sbase = par * 2048
                    for h in range(NH):
                        def gsum(e, xs=xs, h=h, sbase=sbase):
                            b = xs * 4096 + h * 512
                            d = sbase + h * 128
                            e.tensor_tensor(stag[:, d:d + 128], x_sb[:, b:b + 128],
                                            x_sb[:, b + 128:b + 256], op=ALU.add)
                            e.tensor_tensor(stag[:, d:d + 128], stag[:, d:d + 128],
                                            x_sb[:, b + 256:b + 384], op=ALU.add)
                            return e.tensor_tensor(stag[:, d:d + 128], stag[:, d:d + 128],
                                                   x_sb[:, b + 384:b + 512], op=ALU.add)
                        S.add("dve", gsum, reads=[r_xsb[xs]], writes=[r_stag])
                    S.add("act", lambda e, xs=xs, sbase=sbase: e.activation(
                        ybuf[:, xs * 4096:xs * 4096 + 1024], stag[:, sbase:sbase + 1024],
                        AF.Copy), reads=[r_stag], writes=[r_ybf[xs]])
                    ostrip = gbuf if par == 0 else arA
                    r_ostrip = r_gb if par == 0 else r_arA
                    for hg in range(2):
                        bk2 = hg % 4
                        def tiles(e, xs=xs, hg=hg, bk2=bk2):
                            ins = None
                            for i in range(4):
                                h = hg * 4 + i
                                ins = e.matmul(psum[bk2][:, i * 128:(i + 1) * 128],
                                               ybuf[:, xs * 4096 + h * 128:xs * 4096 + (h + 1) * 128],
                                               idb[:], start=(i == 0), stop=(i == 3))
                            return ins
                        S.add("pe", tiles, reads=[r_ybf[xs], r_id], writes=[r_bank[bk2]])
                        S.add("dve", lambda e, hg=hg, bk2=bk2, ostrip=ostrip: e.tensor_copy(
                            ostrip[:, hg * 512:(hg + 1) * 512], psum[bk2][:]),
                            reads=[r_bank[bk2]], writes=[r_ostrip])
                    S.add("gpq", lambda e, o=o, ostrip=ostrip: e.dma_start(
                        wot[o], ostrip[:, 0:1024]), reads=[r_ostrip], writes=[r_wo[o]])
            prep_wo()

            S.finalize(block, sems,
                       final_waits=[(s, S.cnt[s]) for s in DMA_QUEUES + SM_SEMS if S.cnt[s]])

        with nc.Block("main") as block:
            S = Sched(nc, cnt)
            r_bank = [S.res() for _ in range(8)]
            r_x = [S.res() for _ in range(KD)]
            r_ych = [S.res() for _ in range(KD)]
            r_g = S.res()
            r_sq = [S.res() for _ in range(NH)]
            r_slotA = [S.res() for _ in range(3)]
            r_slotB = [S.res() for _ in range(3)]
            r_stag = [S.res() for _ in range(8)]
            r_sqt = [S.res() for _ in range(2)]
            r_rows = S.res()
            slotA_ctr = [0]

            def next_slotA():
                i = slotA_ctr[0] % 3
                slotA_ctr[0] += 1
                return arA[:, i * 4096:(i + 1) * 4096], r_slotA[i]

            def stslot(i):
                return stag[:, i * T:(i + 1) * T], r_stag[i]

            def emit_ln(src_is_x=True):
                S.phase = "ln"
                for k in range(KD):
                    par = k % 2
                    S.add("dve", lambda e, k=k: e.tensor_copy(
                        ybuf[:, k * T:(k + 1) * T], x_sb[:, k * T:(k + 1) * T]),
                        reads=[r_x[k]], writes=[r_ych[k]])
                    S.add("act", lambda e, k=k, par=par: e.activation(
                        sqtmp[:, par * T:(par + 1) * T], ybuf[:, k * T:(k + 1) * T], AF.Square),
                        reads=[r_ych[k]], writes=[r_sqt[par]])
                    S.add("pe", lambda e, k=k: e.matmul(
                        psum[6][0:1, :], onesc[:], ybuf[:, k * T:(k + 1) * T],
                        start=(k == 0), stop=(k == KD - 1)),
                        reads=[r_ych[k]], writes=[r_bank[6]] if k in (0, KD - 1) else [])
                    S.add("pe", lambda e, k=k, par=par: e.matmul(
                        psum[7][0:1, :], onesc[:], sqtmp[:, par * T:(par + 1) * T],
                        start=(k == 0), stop=(k == KD - 1)),
                        reads=[r_sqt[par]], writes=[r_bank[7]] if k in (0, KD - 1) else [])
                S.add("dve", lambda e: e.tensor_scalar(r_a[:], psum[6][0:1, :], -1.0 / D, None,
                                                       op0=ALU.mult),
                      reads=[r_bank[6]], writes=[r_rows])
                S.add("dve", lambda e: e.tensor_scalar(r_b[:], psum[7][0:1, :], 1.0 / D, None,
                                                       op0=ALU.mult),
                      reads=[r_bank[7]], writes=[r_rows])
                S.add("act", lambda e: e.activation(r_c[:], r_a[:], AF.Square),
                      reads=[r_rows], writes=[r_rows])
                S.add("dve", lambda e: e.tensor_tensor(r_b[:], r_b[:], r_c[:], op=ALU.subtract),
                      reads=[r_rows], writes=[r_rows])
                S.add("act", lambda e: e.activation(r_b[:], r_b[:], AF.Sqrt, bias=eps_t[0:1, :]),
                      reads=[r_rows], writes=[r_rows])
                S.add("dve", lambda e: e.reciprocal(r_b[:], r_b[:]),
                      reads=[r_rows], writes=[r_rows])
                S.add("dve", lambda e: e.tensor_tensor(r_a[:], r_a[:], r_b[:], op=ALU.mult),
                      reads=[r_rows], writes=[r_rows])
                S.add("pe", lambda e: e.matmul(psum[6][:], onesr[:], r_b[:], start=True, stop=True),
                      reads=[r_rows], writes=[r_bank[6]])
                S.add("pe", lambda e: e.matmul(psum[7][:], onesr[:], r_a[:], start=True, stop=True),
                      reads=[r_rows], writes=[r_bank[7]])
                for k in range(KD):
                    sl, r_sl = stslot(k % 2)
                    S.add("dve", lambda e, k=k, sl=sl: e.tensor_tensor(
                        sl, x_sb[:, k * T:(k + 1) * T], psum[6][:], op=ALU.mult),
                        reads=[r_x[k], r_bank[6]], writes=[r_sl])
                    S.add("dve", lambda e, k=k, sl=sl: e.tensor_tensor(
                        ybuf[:, k * T:(k + 1) * T], sl, psum[7][:], op=ALU.add),
                        reads=[r_sl, r_bank[7]], writes=[r_ych[k]])

            def emit_ffn():
                for (c0, c1) in QUARTERS:
                    S.phase = "ffnA"
                    nq = c1 - c0
                    for c in range(c0, c1):
                        par = c % 2
                        s1, r_s1 = next_slotA()
                        s3, r_s3 = next_slotA()
                        q1, q2 = ("spq", "gpq") if c % 2 == 0 else ("gpq", "spq")
                        S.add(q1, lambda e, c=c, s1=s1: e.dma_start(s1, w13t[c][:, 0:4096]),
                              writes=[r_s1])
                        S.add(q2, lambda e, c=c, s3=s3: e.dma_start(s3, w13t[c][:, 4096:8192]),
                              writes=[r_s3])
                        b1, b3 = 2 * par, 2 * par + 1
                        def mmh(e, s=None, bk=None):
                            ins = None
                            for k in range(KD):
                                ins = e.matmul(psum[bk][:], s[:, k * 128:(k + 1) * 128],
                                               ybuf[:, k * T:(k + 1) * T],
                                               start=(k == 0), stop=(k == KD - 1))
                            return ins
                        if c < 3:
                            def addfine(s, r_s, bk):
                                for k in range(KD):
                                    def one(e, s=s, bk=bk, k=k):
                                        return e.matmul(psum[bk][:], s[:, k * 128:(k + 1) * 128],
                                                        ybuf[:, k * T:(k + 1) * T],
                                                        start=(k == 0), stop=(k == KD - 1))
                                    S.add("pe", one, reads=[r_s, r_ych[k]],
                                          writes=[r_bank[bk]] if k in (0, KD - 1) else [])
                            addfine(s1, r_s1, b1)
                            addfine(s3, r_s3, b3)
                        else:
                            S.add("pe", lambda e, s1=s1, b1=b1: mmh(e, s=s1, bk=b1),
                                  reads=[r_s1] + r_ych, writes=[r_bank[b1]])
                            S.add("pe", lambda e, s3=s3, b3=b3: mmh(e, s=s3, bk=b3),
                                  reads=[r_s3] + r_ych, writes=[r_bank[b3]])
                        sl, r_sl = stslot(2 + par)
                        S.add("act", lambda e, b1=b1, sl=sl: e.activation(sl, psum[b1][:], AF.Silu),
                              reads=[r_bank[b1]], writes=[r_sl])
                        S.add("dve", lambda e, c=c, c0=c0, b3=b3, sl=sl: e.tensor_tensor(
                            gbuf[:, (c - c0) * T:(c - c0 + 1) * T], sl, psum[b3][:], op=ALU.mult),
                            reads=[r_sl, r_bank[b3]], writes=[r_g])
                    S.phase = "ffnB"
                    for o in range(KD):
                        par = o % 3
                        slot = arB[:, par * 2816:par * 2816 + nq * 128]
                        r_sl = r_slotB[par]
                        q1 = "spq" if o % 2 == 0 else "gpq"
                        S.add(q1, lambda e, o=o, c0=c0, c1=c1, slot=slot: e.dma_start(
                            slot, w2t[o][:, c0 * 128:c1 * 128]), writes=[r_sl])
                        bk = 4 + par
                        def mmo(e, slot=slot, bk=bk, nq=nq):
                            ins = None
                            for i in range(nq):
                                ins = e.matmul(psum[bk][:], slot[:, i * 128:(i + 1) * 128],
                                               gbuf[:, i * T:(i + 1) * T],
                                               start=(i == 0), stop=(i == nq - 1))
                            return ins
                        S.add("pe", mmo, reads=[r_sl, r_g], writes=[r_bank[bk]])
                        S.add("dve", lambda e, o=o, bk=bk: e.tensor_tensor(
                            x_sb[:, o * T:(o + 1) * T], x_sb[:, o * T:(o + 1) * T],
                            psum[bk][:], op=ALU.add),
                            reads=[r_bank[bk]], writes=[r_x[o]])

            def emit_attn(p=-1):
                S.phase = "attn"
                for h in range(NH):
                    par = h % 2
                    slot, r_sl = next_slotA()
                    q1 = "spq" if par == 0 else "gpq"
                    S.add(q1, lambda e, h=h, slot=slot: e.dma_start(slot, wqt[h]),
                          writes=[r_sl])
                    bk = 4 + par
                    def mmq(e, slot=slot, bk=bk):
                        ins = None
                        for k in range(KD):
                            ins = e.matmul(psum[bk][:], slot[:, k * 128:(k + 1) * 128],
                                           ybuf[:, k * T:(k + 1) * T],
                                           start=(k == 0), stop=(k == KD - 1))
                        return ins
                    if h < 3:
                        for k in range(KD):
                            def one(e, slot=slot, bk=bk, k=k):
                                return e.matmul(psum[bk][:], slot[:, k * 128:(k + 1) * 128],
                                                ybuf[:, k * T:(k + 1) * T],
                                                start=(k == 0), stop=(k == KD - 1))
                            S.add("pe", one, reads=[r_sl, r_ych[k]],
                                  writes=[r_bank[bk]] if k in (0, KD - 1) else [])
                    else:
                        S.add("pe", mmq, reads=[r_sl] + r_ych, writes=[r_bank[bk]])
                    smin, r_smin = stslot(0 + par)
                    srel, r_srel = stslot(2 + par)
                    sexp, r_sexp = stslot(4 + par)
                    S.add("dve", lambda e, h=h, bk=bk, smin=smin: e.tensor_scalar(
                        smin, psum[bk][:], bqs[:, h:h + 1], 0.0, op0=ALU.add, op1=ALU.min),
                        reads=[r_bank[bk]], writes=[r_smin])
                    S.add("dve", lambda e, h=h, bk=bk, srel=srel: e.tensor_scalar(
                        srel, psum[bk][:], bqs[:, h:h + 1], 0.0, op0=ALU.add, op1=ALU.max),
                        reads=[r_bank[bk]], writes=[r_srel])
                    S.add("act", lambda e, smin=smin, sexp=sexp: e.activation(sexp, smin, AF.Exp),
                          reads=[r_smin], writes=[r_sexp])
                    S.add("dve", lambda e, h=h, sexp=sexp, srel=srel: e.tensor_tensor(
                        sq[:, h * T:(h + 1) * T], sexp, srel, op=ALU.add),
                        reads=[r_sexp, r_srel], writes=[r_sq[h]])
                tap(p, "sqv", sq[:], [r_sq[h] for h in range(NH)])
                for h in range(NH):
                    par = h % 2
                    bkd = 4 + par
                    S.add("pe", lambda e, h=h: e.matmul(
                        psum[6][:], m_bf[:, h * 128:(h + 1) * 128], sq[:, h * T:(h + 1) * T],
                        start=True, stop=True),
                        reads=[r_sq[h]], writes=[r_bank[6]])
                    S.add("pe", lambda e, h=h, bkd=bkd: e.matmul(
                        psum[bkd][0:1, :], zt[:, h:h + 1], sq[:, h * T:(h + 1) * T],
                        start=True, stop=True),
                        reads=[r_sq[h]], writes=[r_bank[bkd]])
                    S.add("dve", lambda e, bkd=bkd: e.tensor_scalar(
                        r_c[:], psum[bkd][0:1, :], ATT_EPS, None, op0=ALU.add),
                        reads=[r_bank[bkd]], writes=[r_rows])
                    S.add("dve", lambda e: e.reciprocal(r_c[:], r_c[:]),
                          reads=[r_rows], writes=[r_rows])
                    S.add("pe", lambda e: e.matmul(psum[7][:], onesr[:], r_c[:],
                                                   start=True, stop=True),
                          reads=[r_rows], writes=[r_bank[7]])
                    sdr, r_sdr = stslot(6 + par)
                    S.add("dve", lambda e, sdr=sdr: e.tensor_copy(sdr, psum[7][:]),
                          reads=[r_bank[7]], writes=[r_sdr])
                    S.add("dve", lambda e, h=h, sdr=sdr: e.tensor_tensor(
                        sq[:, h * T:(h + 1) * T], psum[6][:], sdr, op=ALU.mult),
                        reads=[r_bank[6], r_sdr], writes=[r_sq[h]])
                tap(p, "retv", sq[:], [r_sq[h] for h in range(NH)])
                for ob in range(8):
                    slot, r_sl = next_slotA()
                    q1 = "spq" if ob % 2 == 0 else "gpq"
                    S.add(q1, lambda e, ob=ob, slot=slot: e.dma_start(
                        slot.rearrange("p (o j) -> p o j", o=4),
                        wot[ob * 4:(ob + 1) * 4].rearrange("o p j -> p o j")),
                        writes=[r_sl])
                    for i in range(4):
                        o = ob * 4 + i
                        bk = 4 + (o % 2)
                        def mmo(e, slot=slot, i=i, bk=bk):
                            ins = None
                            for h in range(NH):
                                ins = e.matmul(psum[bk][:],
                                               slot[:, i * 1024 + h * 128:i * 1024 + (h + 1) * 128],
                                               sq[:, h * T:(h + 1) * T],
                                               start=(h == 0), stop=(h == NH - 1))
                            return ins
                        S.add("pe", mmo, reads=[r_sl] + [r_sq[h] for h in range(NH)],
                              writes=[r_bank[bk]])
                        S.add("dve", lambda e, o=o, bk=bk: e.tensor_tensor(
                            x_sb[:, o * T:(o + 1) * T], x_sb[:, o * T:(o + 1) * T],
                            psum[bk][:], op=ALU.add),
                            reads=[r_bank[bk]], writes=[r_x[o]])

            def tap(p, name, src, reads):
                if p != 0 or name not in dbg:
                    return
                S.add("gpq", lambda e: e.dma_start(dbg[name], src), reads=list(reads))

            def emit_pass(p):
                S.phase = "io"
                tok0 = p * T
                if p == 0:
                    tap(p, "w13t0", w13t[0], [])
                    tap(p, "w2t0", w2t[0], [])
                    tap(p, "wqt0", wqt[0], [])
                    tap(p, "wot0", wot[0], [])
                for tt in range(4):
                    S.add("gpq", lambda e, tt=tt: e.dma_start(
                        stag[:], hs[tok0 + tt * 128: tok0 + (tt + 1) * 128, :]),
                        reads=[r_stag[i] for i in range(8)],
                        writes=[r_stag[i] for i in range(8)])
                    for kg in range(8):
                        bk = 4 + (kg % 2)
                        def trp(e, kg=kg, bk=bk):
                            ins = None
                            for i in range(4):
                                k = kg * 4 + i
                                ins = e.matmul(psum[bk][:, i * 128:(i + 1) * 128],
                                               stag[:, k * 128:(k + 1) * 128], idf[:],
                                               start=(i == 0), stop=(i == 3))
                            return ins
                        S.add("pe", trp, reads=[r_stag[i] for i in range(8)],
                              writes=[r_bank[bk]])
                        for i in range(4):
                            k = kg * 4 + i
                            S.add("dve", lambda e, tt=tt, k=k, i=i, bk=bk: e.tensor_copy(
                                x_sb[:, k * T + tt * 128:k * T + (tt + 1) * 128],
                                psum[bk][:, i * 128:(i + 1) * 128]),
                                reads=[r_bank[bk]], writes=[r_x[k]])
                emit_ln()
                tap(p, "y1", ybuf[:], r_ych)
                emit_ffn()
                tap(p, "xf1", x_sb[:], [r_x[k] for k in range(KD)])
                emit_ln()
                tap(p, "y2", ybuf[:], r_ych)
                emit_attn(p)
                tap(p, "xat", x_sb[:], [r_x[k] for k in range(KD)])
                emit_ln()
                emit_ffn()
                tap(p, "xf2", x_sb[:], [r_x[k] for k in range(KD)])
                S.phase = "io"
                for tt in range(4):
                    for kg in range(8):
                        bk = 4 + (kg % 2)
                        def trp(e, tt=tt, kg=kg, bk=bk):
                            ins = None
                            for i in range(4):
                                k = kg * 4 + i
                                ins = e.matmul(psum[bk][:, i * 128:(i + 1) * 128],
                                               x_sb[:, k * T + tt * 128:k * T + (tt + 1) * 128],
                                               idf[:], start=(i == 0), stop=(i == 3))
                            return ins
                        S.add("pe", trp, reads=[r_x[kg * 4 + i] for i in range(4)],
                              writes=[r_bank[bk]])
                        S.add("dve", lambda e, kg=kg, bk=bk: e.tensor_copy(
                            stag[:, kg * 512:(kg + 1) * 512], psum[bk][:]),
                            reads=[r_bank[bk]], writes=[r_stag[kg]])
                    S.add("gpq" if tt % 2 == 0 else "spq",
                          lambda e, tt=tt: e.dma_start(
                              out_d[tok0 + tt * 128: tok0 + (tt + 1) * 128, :], stag[:]),
                          reads=[r_stag[i] for i in range(8)])

            for p in range(NPASS):
                emit_pass(p)

            S.finalize(block, sems,
                       final_waits=[(s, S.cnt[s]) for s in DMA_QUEUES if S.cnt[s]])

    return nc


WEIGHT_NAMES = [
    "L0_attn_Wq", "L0_attn_bq", "L0_attn_Wo", "L0_attn_M", "L0_attn_z",
    "L0_ffn_w1", "L0_ffn_w3", "L0_ffn_w2",
    "L0_ffn_lora_A1", "L0_ffn_lora_B1", "L0_ffn_lora_A3", "L0_ffn_lora_B3",
    "L0_ffn_lora_A2", "L0_ffn_lora_B2",
]

_NC_CACHE = {}


def kernel(**inputs):
    hs = np.ascontiguousarray(
        np.asarray(inputs["L0_attn_hidden_states"], dtype=np.float32).reshape(NCORES * TOK, D))
    if "nc" not in _NC_CACHE:
        _NC_CACHE["nc"] = build_nc()
    nc = _NC_CACHE["nc"]
    weights = {n: np.ascontiguousarray(np.asarray(inputs[n], np.float32)) for n in WEIGHT_NAMES}
    in_maps = []
    for c in range(NCORES):
        m = dict(weights)
        m["L0_attn_hidden_states"] = np.ascontiguousarray(hs[c * TOK:(c + 1) * TOK])
        in_maps.append(m)
    res = run_bass_kernel_spmd(nc, in_maps, core_ids=list(range(NCORES)))
    out = np.concatenate([np.asarray(r["out"]) for r in res.results], axis=0)
    return out.reshape(2, NCORES * TOK // 2, D)
